# version 02 (final)
# baseline (speedup 1.0000x reference)
import sys
if '/opt/trn_rl_repo' not in sys.path:
    sys.path.insert(0, '/opt/trn_rl_repo')
import numpy as np
import ml_dtypes
import concourse.bass as bass
import concourse.mybir as mybir

F32 = mybir.dt.float32
BF16 = mybir.dt.bfloat16
AF = mybir.ActivationFunctionType
ALU = mybir.AluOpType
BFD = ml_dtypes.bfloat16

D, T = 1024, 2048
NW, WT = 4, 512
NA = D // 128
HV, HK, DK, DV = 16, 8, 64, 64
CL, CPW = 64, 128
NCP = T // CPW
NT = T // 128
SLOT_PERM = [0, 1, 4, 5, 8, 9, 12, 13, 2, 3, 6, 7, 10, 11, 14, 15]

ENGINES = ("pe", "dve", "act", "pool", "sp")
DBG = False


def kh_of(s):
    return SLOT_PERM[s] // 2


def kbase(s):
    return 64 * (kh_of(s) % 2)


def ktile(s):
    return 4 + kh_of(s) // 2


def qtile(s):
    return kh_of(s) // 2


class Op:
    __slots__ = ("eng", "emit", "deps", "seq", "need_inc", "inc_val",
                 "need_drain", "waits", "label", "dma_stream", "dma_idx", "dma_snap")

    def __init__(self, eng, emit, label=""):
        self.eng = eng
        self.emit = emit
        self.deps = []
        self.seq = 0
        self.need_inc = False
        self.inc_val = None
        self.need_drain = False
        self.waits = []
        self.label = label
        self.dma_stream = None
        self.dma_idx = None


class Prog:
    def __init__(self, nc, tag, entry_waits=()):
        self.nc = nc
        self.tag = tag
        self.q = {e: [] for e in ENGINES}
        self.sem = {e: nc.alloc_semaphore(f"sem_{tag}_{e}") for e in ENGINES}
        self.dma = {}
        self.bufw = {}
        self.bufr = {}
        self.entry_waits = list(entry_waits)

    def stream(self, name, queue):
        if name not in self.dma:
            self.dma[name] = [queue, self.nc.alloc_semaphore(f"dma_{self.tag}_{name}"), 0]
        return name

    def _conflict_keys(self, key):
        name, sub = key
        out = [key]
        if sub is None:
            out += [k for k in set(list(self.bufw) + list(self.bufr))
                    if k[0] == name and k[1] is not None]
        else:
            out.append((name, None))
        return out

    def _k(self, x):
        return x if isinstance(x, tuple) else (x, None)

    def _record(self, op, reads, writes):
        deps = []
        for key in reads:
            for k in self._conflict_keys(key):
                w = self.bufw.get(k)
                if w is not None:
                    deps.append(w)
        for key in writes:
            for k in self._conflict_keys(key):
                w = self.bufw.get(k)
                rs = self.bufr.get(k, ())
                if w is not None and not (w.dma_stream is not None and rs):
                    deps.append(w)
                deps.extend(rs)
        seen = set()
        for d in deps:
            if id(d) not in seen and d is not op:
                seen.add(id(d))
                op.deps.append(d)
        for key in reads:
            self.bufr.setdefault(key, []).append(op)
        for key in writes:
            self.bufw[key] = op
            self.bufr[key] = []

    @staticmethod
    def _bankread(key):
        name, sub = key
        if name.startswith("ps"):
            return (name, None)
        return key

    def op(self, eng, emit, reads=(), writes=(), extra_deps=(), label=""):
        op = Op(eng, emit, label)
        op.dma_snap = {k: v[2] for k, v in self.dma.items()}
        self._record(op, [self._bankread(self._k(x)) for x in reads],
                     [self._k(x) for x in writes])
        for d in extra_deps:
            if d is not None and d is not op and d not in op.deps:
                op.deps.append(d)
        self.q[eng].append(op)
        return op

    def dma_op(self, sname, out_ap, in_ap, reads=(), writes=(), label="",
               allow_slow=False):
        queue, sem, issued = self.dma[sname]
        self.dma[sname][2] = issued + 1

        def emit(h, out_ap=out_ap, in_ap=in_ap, sem=sem):
            ins = h.dma_start(out_ap, in_ap, allow_slow_non_contiguous=allow_slow)
            ins.then_inc(sem, 16)
            return ins

        op = Op(queue, emit, label)
        op.dma_snap = {k: v[2] for k, v in self.dma.items()}
        op.dma_stream = sname
        op.dma_idx = issued + 1
        self._record(op, [self._bankread(self._k(x)) for x in reads],
                     [self._k(x) for x in writes])
        self.q[queue].append(op)
        return op

    def finalize(self):
        for e in ENGINES:
            for i, op in enumerate(self.q[e]):
                op.seq = i + 1
        for e in ENGINES:
            for op in self.q[e]:
                for d in op.deps:
                    if d.dma_stream is None and d.eng != e:
                        d.need_inc = True
        import os
        if os.environ.get("KZ_ALLINC"):
            for e in ENGINES:
                for op in self.q[e]:
                    if op.dma_stream is None:
                        op.need_inc = True
        for e in ENGINES:
            for op in reversed(self.q[e]):
                if op.dma_stream is None:
                    op.need_inc = True
                    break
        inc_cnt = {e: 0 for e in ENGINES}
        for e in ENGINES:
            for op in self.q[e]:
                if op.need_inc:
                    inc_cnt[e] += 1
                    op.inc_val = inc_cnt[e]
        self.final_inc = inc_cnt
        for e in ENGINES:
            wm = {}
            drained_upto = 0
            for op in self.q[e]:
                covered = set()
                for d in op.deps:
                    if d.dma_stream is None:
                        for dd in d.deps:
                            if dd.dma_stream is not None:
                                covered.add(id(dd))
                mywaits = {}
                for d in op.deps:
                    if id(d) in covered:
                        continue
                    if d.dma_stream is not None:
                        key = ("dma", d.dma_stream)
                        v = 16 * d.dma_idx
                        if wm.get(key, 0) < v:
                            mywaits[key] = max(mywaits.get(key, 0), v)
                    elif d.eng == e:
                        if d.seq > drained_upto:
                            op.need_drain = True
                    else:
                        key = ("eng", d.eng)
                        v = d.inc_val
                        if wm.get(key, 0) < v:
                            mywaits[key] = max(mywaits.get(key, 0), v)
                for key, v in mywaits.items():
                    wm[key] = v
                    if key[0] == "dma":
                        op.waits.append((self.dma[key[1]][1], v))
                    else:
                        op.waits.append((self.sem[key[1]], v))
                import os as _os
                if _os.environ.get("KZ_NODRAIN"):
                    op.need_drain = False
                if op.need_drain:
                    drained_upto = op.seq - 1
        tail = Op("sp", lambda h: h.nop(), "final_barrier")
        for e in ENGINES:
            if e != "sp" and inc_cnt[e] > 0:
                tail.waits.append((self.sem[e], inc_cnt[e]))
        for sname, (qe, sem, issued) in self.dma.items():
            if issued > 0:
                tail.waits.append((sem, 16 * issued))
        self.q["sp"].append(tail)
        tail.need_inc = True
        inc_cnt["sp"] += 1
        tail.inc_val = inc_cnt["sp"]
        self.final_inc = inc_cnt
        self.exit_waits = [(self.sem["sp"], inc_cnt["sp"])]

    def replay(self):
        nc = self.nc
        with nc.Block() as blk:
            def runner(eng):
                def run(h):
                    for (sem, v) in self.entry_waits:
                        h.wait_ge(sem, v)
                    for op in self.q[eng]:
                        for (sem, v) in op.waits:
                            h.wait_ge(sem, v)
                        if op.need_drain:
                            h.drain()
                        ins = op.emit(h)
                        if op.inc_val is not None:
                            assert ins is not None, f"{eng} {op.label}"
                            ins.then_inc(self.sem[eng], 1)
                return run
            if self.q["sp"]:
                blk.sync(runner("sp"))
            if self.q["pe"]:
                blk.tensor(runner("pe"))
            if self.q["dve"]:
                blk.vector(runner("dve"))
            if self.q["act"]:
                blk.scalar(runner("act"))
            if self.q["pool"]:
                blk.gpsimd(runner("pool"))


class Pools:
    def __init__(self, nc):
        self.nc = nc
        self.t_bf = nc.alloc_sbuf_tensor("pool_bf", [128, 33000], BF16)
        self.t_f32 = nc.alloc_sbuf_tensor("pool_f32", [128, 2600], F32)
        self.t_row = nc.alloc_sbuf_tensor("pool_row", [16, 6200], F32)
        self.mark = {}
        self.reset()

    def reset(self):
        for k in getattr(self, "off", {}):
            self.mark[k] = max(self.mark.get(k, 0), self.off[k])
        self.off = {"bf": 0, "f32": 0, "row": 0}

    def bf(self, cols):
        o = self.off["bf"]
        self.off["bf"] = o + cols
        assert self.off["bf"] <= self.t_bf.shape[1], f"bf pool overflow {self.off}"
        return self.t_bf[:, o:o + cols]

    def f32(self, cols):
        o = self.off["f32"]
        self.off["f32"] = o + cols
        assert self.off["f32"] <= self.t_f32.shape[1], f"f32 pool overflow {self.off}"
        return self.t_f32[:, o:o + cols]

    def row(self, cols):
        o = self.off["row"]
        self.off["row"] = o + cols
        assert self.off["row"] <= self.t_row.shape[1], f"row pool overflow {self.off}"
        return self.t_row[:, o:o + cols]

    def f32b(self, cols):
        o = (self.off["bf"] + 1) // 2 * 2
        self.off["bf"] = o + 2 * cols
        assert self.off["bf"] <= self.t_bf.shape[1], f"bf pool overflow {self.off}"
        return self.t_bf[:, o:o + 2 * cols].bitcast(F32)


WPOOL_COLS = 8 * 4160 + 200


def make_shared(nc, p: Prog):
    st = {}
    st["ones1"] = nc.alloc_sbuf_tensor("c_ones1", [128, 1], BF16)
    st["onesr"] = nc.alloc_sbuf_tensor("c_onesr", [1, 128], BF16)
    st["ident"] = nc.alloc_sbuf_tensor("c_ident", [128, 128], BF16)
    st["ps"] = [nc.alloc_psum_tensor(f"psum{i}", [128, 512], F32) for i in range(7)]
    p.op("dve", lambda h: h.memset(st["ones1"][:], 1.0), writes=["c_ones1"])
    p.op("dve", lambda h: h.memset(st["onesr"][:], 1.0), writes=["c_onesr"])
    st["warm"] = nc.alloc_sbuf_tensor("c_warm", [128, 256], F32)
    p.op("dve", lambda h: h.memset(st["ident"][:], 0.0), writes=["c_ident"])
    p.op("pe", lambda h: h.matmul(st["ps"][0][:, 0:128], st["ident"][:], st["ident"][:],
                                  start=True, stop=True),
         reads=["c_ident"], writes=["ps0"], label="warmmm")
    p.op("dve", lambda h: h.tensor_copy(st["warm"][:, 0:128], st["ps"][0][:, 0:128]),
         reads=["ps0"], writes=["c_warm"], label="warmdve")
    p.op("act", lambda h: h.activation(st["warm"][:, 128:256], st["ps"][0][:, 0:128],
                                       mybir.ActivationFunctionType.Exp),
         reads=["ps0"], writes=["c_warm"], label="warmact")
    return st


def ln_pass(nc, p, st, po, xin_ap, nm, dst_dram=None, dst_sbuf=None, dst_put=None):
    ones1, onesr, PS = st["ones1"], st["onesr"], st["ps"]
    sx = p.stream(nm + "x", "sp")
    xf = st["gxf"]
    xb = st["gxb"]
    sq = xb
    rowf = st["growf"]
    rowbf = st["growbf"]
    rowlo = st["growlo"]
    xc = st["gxc"]
    for w in range(NW):
        c0 = w * WT
        p.dma_op(sx, xf[:].rearrange("p (a t) -> p a t", a=NA),
                 xin_ap[:, c0:c0 + WT].rearrange("(a q) t -> q a t", q=128),
                 writes=["gxf"], label=f"{nm}xin{w}")
        p.op("act", lambda h: h.copy(xb[:], xf[:]), reads=["gxf"], writes=["gxb"])
        for a in range(NA):
            p.op("pe", lambda h, a=a: h.matmul(
                PS[0][0:1, :], ones1[:], xb[:, a * WT:(a + 1) * WT],
                start=(a == 0), stop=(a == NA - 1)), reads=["gxb", "c_ones1"], writes=[("ps0", "s")])
        p.op("dve", lambda h: h.tensor_mul(sq[:], xb[:], xb[:]), reads=["gxb"], writes=["gxb"])
        for a in range(NA):
            p.op("pe", lambda h, a=a: h.matmul(
                PS[0][32:33, :], ones1[:], sq[:, a * WT:(a + 1) * WT],
                start=(a == 0), stop=(a == NA - 1)), reads=["gxb", "c_ones1"], writes=[("ps0", "s")])
        R, RB, RL = rowf, rowbf, rowlo
        p.op("dve", lambda h: h.tensor_scalar_mul(R[0:1, 0:WT], PS[0][0:1, :], 1.0 / D),
             reads=[("ps0", "s")], writes=[("growf", 0)])
        p.op("dve", lambda h: h.tensor_scalar_mul(R[0:1, WT:2 * WT], PS[0][32:33, :], 1.0 / D),
             reads=[("ps0", "s")], writes=[("growf", 1)])
        p.op("dve", lambda h: h.tensor_mul(R[0:1, 2 * WT:3 * WT], R[0:1, 0:WT], R[0:1, 0:WT]),
             reads=[("growf", 0)], writes=[("growf", 2)])
        p.op("dve", lambda h: h.tensor_sub(R[0:1, WT:2 * WT], R[0:1, WT:2 * WT], R[0:1, 2 * WT:3 * WT]),
             reads=[("growf", 1), ("growf", 2)], writes=[("growf", 1)])
        p.op("dve", lambda h: h.tensor_scalar_add(R[0:1, WT:2 * WT], R[0:1, WT:2 * WT], 1e-5),
             reads=[("growf", 1)], writes=[("growf", 1)])
        p.op("dve", lambda h: h.reciprocal(R[0:1, 2 * WT:3 * WT], R[0:1, WT:2 * WT]),
             reads=[("growf", 1)], writes=[("growf", 2)])
        p.op("act", lambda h: h.activation(R[0:1, 2 * WT:3 * WT], R[0:1, 2 * WT:3 * WT], AF.Sqrt),
             reads=[("growf", 2)], writes=[("growf", 2)])
        p.op("dve", lambda h: h.tensor_copy(RB[0:1, 0:WT], R[0:1, 0:WT]),
             reads=[("growf", 0)], writes=[("growbf", 0)])
        p.op("dve", lambda h: h.tensor_sub(RL[0:1, :], R[0:1, 0:WT], RB[0:1, 0:WT]),
             reads=[("growf", 0), ("growbf", 0)], writes=["growlo"])
        p.op("dve", lambda h: h.tensor_copy(RB[0:1, WT:2 * WT], RL[0:1, :]),
             reads=["growlo"], writes=[("growbf", 1)])
        p.op("dve", lambda h: h.tensor_copy(RB[0:1, 2 * WT:3 * WT], R[0:1, 2 * WT:3 * WT]),
             reads=[("growf", 2)], writes=[("growbf", 2)])
        p.op("dve", lambda h: h.tensor_sub(RL[0:1, :], R[0:1, 2 * WT:3 * WT], RB[0:1, 2 * WT:3 * WT]),
             reads=[("growf", 2), ("growbf", 2)], writes=["growlo"])
        p.op("dve", lambda h: h.tensor_copy(RB[0:1, 3 * WT:4 * WT], RL[0:1, :]),
             reads=["growlo"], writes=[("growbf", 3)])
        p.op("pe", lambda h: h.matmul(PS[1][:], onesr[:], RB[0:1, 0:WT], start=True, stop=False),
             reads=[("growbf", 0), "c_onesr"], writes=["ps1"])
        p.op("pe", lambda h: h.matmul(PS[1][:], onesr[:], RB[0:1, WT:2 * WT], start=False, stop=True),
             reads=[("growbf", 1), "c_onesr"], writes=["ps1"])
        p.op("pe", lambda h: h.matmul(PS[2][:], onesr[:], RB[0:1, 2 * WT:3 * WT], start=True, stop=False),
             reads=[("growbf", 2), "c_onesr"], writes=["ps2"])
        p.op("pe", lambda h: h.matmul(PS[2][:], onesr[:], RB[0:1, 3 * WT:4 * WT], start=False, stop=True),
             reads=[("growbf", 3), "c_onesr"], writes=["ps2"])
        for a in range(NA):
            p.op("dve", lambda h, a=a: h.tensor_sub(
                xc[:], xf[:, a * WT:(a + 1) * WT], PS[1][:]),
                reads=["gxf", "ps1"], writes=["gxc"])
            if dst_sbuf is not None:
                p.op("dve", lambda h, a=a: h.tensor_mul(
                    dst_sbuf[:, a * (WT + 1) + 1:(a + 1) * (WT + 1)], xc[:], PS[2][:]),
                    reads=["gxc", "ps2"], writes=[(nm + "xs", a)])
            else:
                p.op("dve", lambda h, a=a: h.tensor_mul(
                    xb[:, a * WT:(a + 1) * WT], xc[:], PS[2][:]),
                    reads=["gxc", "ps2"], writes=[("gxb", ("ln", a))])
        if dst_dram is not None:
            p.dma_op(sx, dst_dram[:, c0:c0 + WT].rearrange("(a q) t -> q a t", q=128),
                     xb[:].rearrange("p (a t) -> p a t", a=NA),
                     reads=["gxb"], label=f"{nm}xsout{w}")
        if dst_put is not None:
            dst_put(w)
    return xf


def build_ffn(nc, p, st, po, xin_ap, xout_ap, wr_ap, wk_ap, wv_ap, murk_ap,
              wpool, nm):
    ones1, onesr, PS = st["ones1"], st["onesr"], st["ps"]
    sw = [p.stream(nm + "w0", "act"), p.stream(nm + "w1", "act")]
    sx = p.stream(nm + "x2", "sp")
    so = p.stream(nm + "o", "sp")
    stg = [po.f32(1024) for _ in range(2)]
    n = 0
    for wi, wap, base in (("r", wr_ap, 0), ("k", wk_ap, 8192), ("v", wv_ap, 16384)):
        for kt in range(8):
            s_ = stg[n % 2]
            skey = (nm + "stg", n % 2)
            p.dma_op(sw[n % 2], s_, wap[kt * 128:(kt + 1) * 128, :], writes=[skey],
                     label=f"{nm}wl{wi}{kt}")
            p.op("pool", lambda h, s_=s_, kt=kt, base=base: h.tensor_copy(
                wpool[:, base + kt * 1024: base + (kt + 1) * 1024], s_),
                reads=[skey], writes=[("wpool", (nm, wi, kt))], label=f"{nm}wc{wi}{kt}")
            n += 1

    xs = po.bf(NA * (WT + 1))
    dlt = po.bf(NA * WT)
    xr = po.bf(NA * WT)
    xk = po.bf(NA * WT)
    rg = po.bf(NA * WT)
    kr = po.bf(NA * WT)
    ka = po.bf(NA * WT)
    tmp = po.bf(NA * WT)
    murk = po.f32(2 * NA)
    p.dma_op(sx, murk[:, 0:NA], murk_ap[0:1, :].rearrange("one (a q) -> (one q) a", q=128),
             writes=[nm + "murk"], allow_slow=True)
    p.dma_op(sx, murk[:, NA:2 * NA], murk_ap[1:2, :].rearrange("one (a q) -> (one q) a", q=128),
             writes=[nm + "murk"], allow_slow=True)
    p.op("dve", lambda h: h.memset(xs[:, :], 0.0), writes=[nm + "xs"])
    xf = st["gxf"]

    def after_ln(w):
        c0 = w * WT
        for a in range(NA):
            b0 = a * (WT + 1)
            p.op("dve", lambda h, a=a, b0=b0: h.tensor_sub(
                dlt[:, a * WT:(a + 1) * WT], xs[:, b0 + 1:b0 + 1 + WT], xs[:, b0:b0 + WT]),
                reads=[(nm + "xs", a), (nm + "xs", None)], writes=[(nm + "dlt", a)])
            p.op("dve", lambda h, a=a, b0=b0: h.scalar_tensor_tensor(
                xr[:, a * WT:(a + 1) * WT], dlt[:, a * WT:(a + 1) * WT],
                murk[:, a:a + 1], xs[:, b0:b0 + WT], ALU.mult, ALU.add),
                reads=[(nm + "dlt", a), (nm + "xs", a), nm + "murk"], writes=[(nm + "xr", a)])
            p.op("dve", lambda h, a=a, b0=b0: h.scalar_tensor_tensor(
                xk[:, a * WT:(a + 1) * WT], dlt[:, a * WT:(a + 1) * WT],
                murk[:, NA + a:NA + a + 1], xs[:, b0:b0 + WT], ALU.mult, ALU.add),
                reads=[(nm + "dlt", a), (nm + "xs", a), nm + "murk"], writes=[(nm + "xk", a)])
        xsv = xs[:].rearrange("p (a v) -> p a v", v=WT + 1)
        p.op("dve", lambda h, xsv=xsv: h.tensor_copy(xsv[:, :, 0:1], xsv[:, :, WT:WT + 1]),
             reads=[(nm + "xs", None)], writes=[(nm + "xs", None)])

        def proj(wbase, src, srckey, evac, lbl):
            for dt_ in range(NA):
                bi = 3 + dt_ % 4
                for kt in range(8):
                    p.op("pe", lambda h, dt_=dt_, kt=kt, wbase=wbase, src=src, bi=bi: h.matmul(
                        PS[bi][:], wpool[:, wbase + kt * 1024 + dt_ * 128: wbase + kt * 1024 + dt_ * 128 + 128],
                        src[:, kt * WT:(kt + 1) * WT], start=(kt == 0), stop=(kt == 7)),
                        reads=[("wpool", (nm, lbl, kt)), (srckey, kt)], writes=[f"ps{bi}"],
                        label=f"{nm}{lbl}{w}_{dt_}{kt}")
                evac(dt_)

        def evac_r(dt_):
            bi = 3 + dt_ % 4
            p.op("act", lambda h, dt_=dt_, bi=bi: h.activation(
                rg[:, dt_ * WT:(dt_ + 1) * WT], PS[bi][:], AF.Sigmoid),
                reads=[f"ps{bi}"], writes=[(nm + "rg", dt_)])

        def evac_k(dt_):
            bi = 3 + dt_ % 4
            p.op("act", lambda h, dt_=dt_, bi=bi: h.activation(
                kr[:, dt_ * WT:(dt_ + 1) * WT], PS[bi][:], AF.Relu),
                reads=[f"ps{bi}"], writes=[(nm + "kr", dt_)])
            p.op("dve", lambda h, dt_=dt_: h.tensor_mul(
                ka[:, dt_ * WT:(dt_ + 1) * WT], kr[:, dt_ * WT:(dt_ + 1) * WT],
                kr[:, dt_ * WT:(dt_ + 1) * WT]),
                reads=[(nm + "kr", dt_)], writes=[(nm + "ka", dt_)])

        def evac_v(dt_):
            bi = 3 + dt_ % 4
            p.op("dve", lambda h, dt_=dt_, bi=bi: h.tensor_mul(
                tmp[:, dt_ * WT:(dt_ + 1) * WT], PS[bi][:], rg[:, dt_ * WT:(dt_ + 1) * WT]),
                reads=[f"ps{bi}", (nm + "rg", dt_)], writes=[(nm + "tmp", dt_)])
            p.op("dve", lambda h, dt_=dt_: h.scalar_tensor_tensor(
                xf[:, dt_ * WT:(dt_ + 1) * WT], tmp[:, dt_ * WT:(dt_ + 1) * WT],
                0.5, xf[:, dt_ * WT:(dt_ + 1) * WT], ALU.mult, ALU.add),
                reads=[(nm + "tmp", dt_), "gxf"], writes=["gxf"])

        proj(0, xr, nm + "xr", evac_r, "r")
        proj(8192, xk, nm + "xk", evac_k, "k")
        proj(16384, ka, nm + "ka", evac_v, "v")
        p.dma_op(so, xout_ap[:, c0:c0 + WT].rearrange("(a q) t -> q a t", q=128),
                 xf[:].rearrange("p (a t) -> p a t", a=NA),
                 reads=["gxf"], label=f"{nm}xo{w}")

    ln_pass(nc, p, st, po, xin_ap, nm, dst_sbuf=xs, dst_put=after_ln)




SLT = 256
NSL = T // SLT
CPS = SLT // CPW


def build_attn(nc, p, st, po, xin_ap, xout_ap, xs2d_ap, wq_ap, wz_ap, wab_ap,
               wout_ap, convw_ap, dtb_ap, alog_ap, cst_ap, wpool, nm="at"):
    ones1, onesr, PS = st["ones1"], st["onesr"], st["ps"]
    sw = [p.stream(nm + "w0", "act"), p.stream(nm + "w1", "act")]
    sx = p.stream(nm + "x2", "sp")
    sxr = p.stream(nm + "xr", "sp")
    so = p.stream(nm + "o", "sp")
    sdb = p.stream(nm + "db", "sp")
    gxf = st["gxf"]

    WQ, WZ, WAB = 0, 16384, 24576
    wouth = st["wouth"]
    stgf = gxf[:, 0:1024]

    def loadw(wname, wap, dest_fn, wc, nk=8):
        for kt in range(nk):
            p.dma_op(sw[kt % 2], stgf[:, 0:wc], wap[kt * 128:(kt + 1) * 128, :],
                     writes=["gxf"], label=f"{nm}wl{wname}{kt}")
            p.op("pool", lambda h, kt=kt, wc=wc, dest_fn=dest_fn: h.tensor_copy(
                dest_fn(kt), stgf[:, 0:wc]),
                reads=["gxf"], writes=[("wpool", (nm, wname, kt))],
                label=f"{nm}wc{wname}{kt}")

    import os
    if "w" not in os.environ.get("KZ_PRE", ""):
        for kt in range(8):
            for hh in range(2):
                p.dma_op(sw[hh], stgf[:, 0:1024], wq_ap[kt * 128:(kt + 1) * 128, hh * 1024:(hh + 1) * 1024],
                         writes=["gxf"], label=f"{nm}wlq{kt}{hh}")
                p.op("pool", lambda h, kt=kt, hh=hh: h.tensor_copy(
                    wpool[:, WQ + kt * 2048 + hh * 1024: WQ + kt * 2048 + (hh + 1) * 1024],
                    stgf[:, 0:1024]),
                    reads=["gxf"], writes=[("wpool", (nm, "qkv", kt))], label=f"{nm}wcq{kt}{hh}")
        loadw("z", wz_ap, lambda kt: wpool[:, WZ + kt * 1024: WZ + (kt + 1) * 1024], 1024)
        loadw("ab", wab_ap, lambda kt: wpool[:, WAB + kt * 64: WAB + (kt + 1) * 64], 64)
        loadw("out", wout_ap, lambda kt: wouth[:, kt * 1024:(kt + 1) * 1024], 1024)

    maskA = po.f32(128)
    maskP = po.f32(128)
    nwrep = po.f32(64)
    cwc = po.f32(64)
    p.dma_op(sx, maskA, cst_ap[:, 0:128], writes=[nm + "maskA"])
    p.dma_op(sx, maskP, cst_ap[:, 128:256], writes=[nm + "maskP"])
    p.dma_op(sx, nwrep, cst_ap[:, 256:320], writes=[nm + "nwrep"])
    p.dma_op(sx, cwc, cst_ap[:, 320:384], writes=[nm + "cwc"])
    selsA = po.bf(128)
    selsP = po.bf(128)
    selsq = po.bf(64)
    p.dma_op(sx, selsA[0:6, :], cst_ap[0:6, 384:448].bitcast(BF16), writes=[nm + "selA"])
    p.dma_op(sx, selsP[0:6, :], cst_ap[0:6, 448:512].bitcast(BF16), writes=[nm + "selP"])
    p.dma_op(sx, selsq[:, :], cst_ap[:, 512:544].bitcast(BF16), writes=[nm + "selsq"])
    p.dma_op(sx, st["ident"][:], cst_ap[:, 544:608].bitcast(BF16), writes=["c_ident"])
    p.dma_op(sx, st["identf"][:], cst_ap[:, 608:736], writes=["c_identf"])
    cwb = po.bf(64)
    cdiag = po.bf(4 * 128)
    _dn = po.row(2)
    dtb = _dn[0:16, 0:1]
    nega = _dn[0:16, 1:2]
    p.dma_op(sx, dtb, dtb_ap[:].rearrange("(s one) -> s one", one=1), writes=[nm + "dtb"])
    p.dma_op(sx, nega, alog_ap[:].rearrange("(s one) -> s one", one=1), writes=[nm + "nega"])
    ck = [nm + k for k in ("maskA", "maskP", "nwrep", "cwc", "selA", "selP", "selsq",
                           "dtb", "nega")] + ["c_ident", "c_identf"]
    if not os.environ.get("KZ_NOFENCE"):
        p.op("dve", lambda h: h.engine_nop(), reads=ck, writes=ck, label=nm + "cfence")
    p.op("dve", lambda h: h.tensor_copy(cwb, cwc), reads=[nm + "cwc"], writes=[nm + "cwb"])
    p.op("act", lambda h: h.activation(nega, nega, AF.Exp), reads=[nm + "nega"], writes=[nm + "nega"])
    p.op("dve", lambda h: h.tensor_scalar_mul(nega, nega, -1.0), reads=[nm + "nega"], writes=[nm + "nega"])

    import os
    if "l" not in os.environ.get("KZ_PRE", ""):
        ln_pass(nc, p, st, po, xin_ap, nm + "L", dst_dram=xs2d_ap)

    return dict(locals())



def build_attn_slabs(nc, p, st, po, L, xin_ap, xout_ap, xs2d_ap, nm="at"):
    ones1, onesr, PS, psb = st["ones1"], st["onesr"], st["ps"], st["psb"]
    ident = st["ident"]
    gxf = st["gxf"]
    gxb = st["gxb"]
    wpool = L["wpool"]
    WQ, WZ, WAB = L["WQ"], L["WZ"], L["WAB"]
    sx, so, sw, sxr = L["sx"], L["so"], L["sw"], L["sxr"]
    maskA, maskP, nwrep = L["maskA"], L["maskP"], L["nwrep"]
    selsA, selsP, selsq = L["selsA"], L["selsP"], L["selsq"]
    cdiag = L["cdiag"]
    cwb = L["cwb"]
    dtb, nega = L["dtb"], L["nega"]

    xsl2 = gxb[:, 0:2048]
    qkvC = po.bf(16 * SLT)
    mixb = [po.bf(SLT + 3) for _ in range(2)]
    bnd = po.bf(16 * 3)
    sqk = po.bf(2 * SLT)
    k_tm = po.bf(2 * 512)
    vbeta = po.bf(2 * 1024)
    scolb = po.bf(2 * 96)
    scolf = po.f32b(2 * 128)
    tile6 = po.bf(4 * 128)
    dA = po.bf(512)
    dP = po.bf(512)
    Ebuf = po.f32(512)
    Mt = po.bf(512)
    Nt = po.bf(512)
    Tt = po.bf(16 * 128)
    Pt = po.bf(16 * 128)
    ch = {k: po.bf(512) for k in ("Mc", "Nc", "Pc", "Mn", "Nn", "Pn")}
    szb = po.f32b(1024)
    scr1 = po.bf(1024)
    r0b = scr1
    ub = po.bf(16 * 64)
    scr2 = po.bf(1024)
    uwb = scr2
    onbb = scr2
    oscr = po.f32b(1024)
    osq = oscr
    onb = oscr
    ost = po.f32b(16 * 64)
    ontp = po.bf(8 * SLT)
    rsq = po.f32(16)
    afrep = po.f32(512)
    Sf = po.f32(512)
    Sb = po.bf(512)
    rowbb = po.bf(SLT)[0:16, :]
    rowseg = {k: po.row(SLT)[0:16, :] for k in
              ("xg", "beta", "lnb", "g", "lg", "rt1", "lnrk", "lnrq",
               "RA", "CA", "RP", "nabrn", "aq", "wbar", "afA", "afB",
               "CAh", "CAm", "CAl", "RPh", "RPm", "RPl", "spf")}
    zrow = po.row(SLT)[0:16, :]
    carry = po.row(8)[0:16, :]
    offs64 = po.row(4)[0:16, :]
    cb4 = po.row(4)[0:16, :]
    p.op("dve", lambda h: h.memset(zrow, 0.0), writes=[nm + "zrow"])
    p.op("dve", lambda h: h.memset(carry[:, 0:2], 0.0), writes=[(nm + "carry", "c")])
    p.op("dve", lambda h: h.memset(Sf[:, :], 0.0), writes=[nm + "Sf"])
    p.op("dve", lambda h: h.memset(Sb[:, :], 0.0), writes=[nm + "Sb"])
    p.op("dve", lambda h: h.memset(bnd[:, :], 0.0), writes=[nm + "bnd"])

    QBF = ["CAh", "CAm", "CAl", "RPh", "RPm", "RPl", "beta", "nabrn", "wbar",
           "aq", "afA", "afB"]

    def scol_b(tt, qi, s):
        return (tt % 2) * 96 + qi * 16 + s

    def scol_f(tt, qi, s):
        return (tt % 2) * 128 + qi * 16 + s

    import os
    ST = int(os.environ.get("KZ_ST", "9"))
    for sl in range(int(os.environ.get("KZ_NSL", str(NSL)))):
        s0 = sl * SLT
        if os.environ.get("KZ_ONLYCP"):
            attn_cp(nc, p, st, po, L, locals(), sl * 2, 0, sl)
            continue
        if os.environ.get("KZ_XSTREAM"):
            p.dma_op(sxr, xsl2[:].rearrange("p (a t) -> p a t", a=NA),
                     xs2d_ap[:, s0:s0 + SLT].rearrange("(a q) t -> q a t", q=128),
                     writes=["gxb"], label=f"{nm}xs2r{sl}")
        elif os.environ.get("KZ_XSFLAT"):
            p.dma_op(sx, xsl2[:, 0:2048], xs2d_ap[0:128, 0:2048],
                     writes=["gxb"], label=f"{nm}xs2r{sl}")
        elif os.environ.get("KZ_XS", "1") == "1":
            p.dma_op(sx, xsl2[:].rearrange("p (a t) -> p a t", a=NA),
                     xs2d_ap[:, s0:s0 + SLT].rearrange("(a q) t -> q a t", q=128),
                     writes=["gxb"], label=f"{nm}xs2r{sl}")
        for ct in range(int(os.environ.get("KZ_CT", "16"))):
            mb = mixb[ct % 2]
            mkey = (nm + "mixb", ct % 2)
            for kt in range(8):
                p.op("pe", lambda h, ct=ct, kt=kt: h.matmul(
                    PS[0][:, 0:SLT], wpool[:, WQ + kt * 2048 + ct * 128: WQ + kt * 2048 + (ct + 1) * 128],
                    xsl2[:, kt * SLT:(kt + 1) * SLT], start=(kt == 0), stop=(kt == 7)),
                    reads=[("wpool", (nm, "qkv", kt)), "gxb"], writes=["ps0"],
                    label=f"{nm}qk{sl}_{ct}{kt}")
            p.op("dve", lambda h, mb=mb: h.tensor_copy(mb[:, 3:3 + SLT], PS[0][:, 0:SLT]),
                 reads=["ps0"], writes=[mkey], label=f"{nm}mev{sl}{ct}")
            p.op("dve", lambda h, mb=mb, ct=ct: h.tensor_copy(mb[:, 0:3], bnd[:, ct * 3:ct * 3 + 3]),
                 reads=[(nm + "bnd", ct)], writes=[mkey])
            for i in range(4):
                p.op("pool", lambda h, ct=ct, i=i: h.affine_select(
                    cdiag[:, i * 128:(i + 1) * 128],
                    cwb[:, ct * 4 + i:ct * 4 + i + 1].broadcast_to([128, 128]),
                    [[-1, 128]], ALU.is_equal, 0.0, base=0, channel_multiplier=1),
                    reads=[nm + "cwb"], writes=[(nm + "cdiag", i)], label=f"{nm}cd{sl}{ct}{i}")
            for i in range(4):
                p.op("pe", lambda h, ct=ct, i=i, mb=mb: h.matmul(
                    PS[1][:, 0:SLT], cdiag[:, i * 128:(i + 1) * 128],
                    mb[:, i:i + SLT], start=(i == 0), stop=(i == 3)),
                    reads=[(nm + "cdiag", i), mkey], writes=["ps1"],
                    label=f"{nm}cv{sl}_{ct}{i}")
            p.op("dve", lambda h, mb=mb, ct=ct: h.tensor_copy(bnd[:, ct * 3:ct * 3 + 3], mb[:, SLT:SLT + 3]),
                 reads=[mkey], writes=[(nm + "bnd", ct)])
            p.op("act", lambda h, ct=ct: h.activation(
                qkvC[:, ct * SLT:(ct + 1) * SLT], PS[1][:, 0:SLT], AF.Silu),
                reads=["ps1"], writes=[(nm + "qkvC", ct)], label=f"{nm}silu{sl}{ct}")
            if L.get("dbg"):
                p.dma_op(L["sdb"], L["dbg"]["dbgq"][:, ct * T + s0: ct * T + s0 + SLT],
                         qkvC[:, ct * SLT:(ct + 1) * SLT], reads=[(nm + "qkvC", ct)])
        if L.get("dbg"):
            qk_ = [(nm + "qkvC", c) for c in range(16)]
            p.op("dve", lambda h: h.engine_nop(), reads=qk_, writes=qk_, label=f"{nm}dbf{sl}")
        if ST < 2:
            if os.environ.get("KZ_TAILCP"):
                attn_cp(nc, p, st, po, L, locals(), sl * 2, 0, sl)
            continue
        ABM = int(os.environ.get("KZ_ABM", "48"))
        for kt in range(8 if os.environ.get("KZ_AB", "1") == "1" else 0):
            p.op("pe", lambda h, kt=kt: h.matmul(
                PS[3][0:ABM, 0:SLT], wpool[:, WAB + kt * 64: WAB + kt * 64 + ABM],
                xsl2[:, kt * SLT:(kt + 1) * SLT], start=(kt == 0), stop=(kt == 7)),
                reads=[("wpool", (nm, "ab", kt)), "gxb"], writes=["ps3"],
                label=f"{nm}ab{sl}{kt}")
        def s2gate(tag, _S=None):
            if os.environ.get("KZ_S2X") == tag:
                if os.environ.get("KZ_TAILCP"):
                    attn_cp(nc, p, st, po, L, _S, sl * 2, 0, sl)
                return True
            return False
        if os.environ.get("KZ_S2AB"):
            if os.environ.get("KZ_TAILCP"):
                attn_cp(nc, p, st, po, L, locals(), sl * 2, 0, sl)
            continue
        R = rowseg
        p.op("dve", lambda h: h.tensor_scalar(R["xg"], PS[3][0:16, 0:SLT], dtb, None, ALU.add),
             reads=["ps3", nm + "dtb"], writes=[(nm + "xg", sl)])
        p.op("act", lambda h: h.activation(R["beta"], PS[3][32:48, 0:SLT], AF.Sigmoid),
             reads=["ps3"], writes=[(nm + "beta", sl)])
        p.op("act", lambda h: h.activation(R["lnb"], R["beta"], AF.Ln),
             reads=[(nm + "beta", sl)], writes=[(nm + "lnb", sl)])
        if s2gate("a", locals()):
            continue
        p.op("act", lambda h: h.activation(R["rt1"], R["xg"], AF.Abs),
             reads=[(nm + "xg", sl)], writes=[(nm + "rt1", sl)])
        p.op("act", lambda h: h.activation(R["rt1"], R["rt1"], AF.Exp, scale=-1.0),
             reads=[(nm + "rt1", sl)], writes=[(nm + "rt1", sl)])
        p.op("dve", lambda h: h.tensor_scalar_add(R["rt1"], R["rt1"], 1.0),
             reads=[(nm + "rt1", sl)], writes=[(nm + "rt1", sl)])
        p.op("act", lambda h: h.activation(R["rt1"], R["rt1"], AF.Ln),
             reads=[(nm + "rt1", sl)], writes=[(nm + "rt1", sl)])
        p.op("dve", lambda h: h.tensor_scalar_max(R["g"], R["xg"], 0.0),
             reads=[(nm + "xg", sl)], writes=[(nm + "g", sl)])
        p.op("dve", lambda h: h.tensor_add(R["g"], R["g"], R["rt1"]),
             reads=[(nm + "g", sl), (nm + "rt1", sl)], writes=[(nm + "g", sl)])
        p.op("dve", lambda h: h.tensor_scalar(R["g"], R["g"], nega, None, ALU.mult),
             reads=[(nm + "g", sl), nm + "nega"], writes=[(nm + "g", sl)])
        if s2gate("b", locals()):
            continue
        p.op("dve", lambda h: h.tensor_tensor_scan(R["lg"], R["g"], zrow, carry[:, 0:1],
                                                   ALU.add, ALU.add),
             reads=[(nm + "g", sl), nm + "zrow", (nm + "carry", "c")], writes=[(nm + "lg", sl)])
        p.op("dve", lambda h: h.tensor_copy(carry[:, 2:3], carry[:, 0:1]),
             reads=[(nm + "carry", "c")], writes=[(nm + "carry", "o")])
        p.op("dve", lambda h: h.tensor_copy(carry[:, 3:4], R["lg"][:, 127:128]),
             reads=[(nm + "lg", sl)], writes=[(nm + "carry", "o")])
        p.op("dve", lambda h: h.tensor_copy(offs64, R["lg"][:, 63::64]),
             reads=[(nm + "lg", sl)], writes=[nm + "offs64"])
        p.op("dve", lambda h: h.tensor_sub(carry[:, 4:5], offs64[:, 0:1], carry[:, 1:2]),
             reads=[nm + "offs64", (nm + "carry", "c")], writes=[(nm + "carry", "af")])
        p.op("dve", lambda h: h.tensor_sub(carry[:, 5:8], offs64[:, 1:4], offs64[:, 0:3]),
             reads=[nm + "offs64"], writes=[(nm + "carry", "af")])
        p.op("act", lambda h: h.activation(carry[:, 4:8], carry[:, 4:8], AF.Exp),
             reads=[(nm + "carry", "af")], writes=[(nm + "carry", "af")])
        p.op("dve", lambda h: h.tensor_copy(carry[:, 0:1], R["lg"][:, 255:256]),
             reads=[(nm + "lg", sl)], writes=[(nm + "carry", "c")])
        p.op("dve", lambda h: h.tensor_copy(carry[:, 1:2], offs64[:, 3:4]),
             reads=[nm + "offs64"], writes=[(nm + "carry", "c")])
        if s2gate("c", locals()):
            continue
        p.op("dve", lambda h: h.tensor_sub(
            R["lg"].rearrange("p (c t) -> p c t", t=128),
            R["lg"].rearrange("p (c t) -> p c t", t=128),
            carry[:, 2:4].rearrange("p (c one) -> p c one", one=1).broadcast_to([16, 2, 128])),
            reads=[(nm + "lg", sl), (nm + "carry", "o")], writes=[(nm + "lg", sl)])
        for blk4, qname in ((0, "lnrq"), (4, "lnrk")):
            for hlf in range(2):
                p.op("dve", lambda h, blk4=blk4, hlf=hlf: h.tensor_mul(
                    sqk[:, :], qkvC[:, (blk4 + 2 * hlf) * SLT:(blk4 + 2 * hlf + 2) * SLT],
                    qkvC[:, (blk4 + 2 * hlf) * SLT:(blk4 + 2 * hlf + 2) * SLT]),
                    reads=[(nm + "qkvC", None)], writes=[nm + "sqk"])
                for k2 in range(2):
                    kt = 2 * hlf + k2
                    p.op("pe", lambda h, kt=kt, k2=k2: h.matmul(
                        PS[3][0:16, 0:SLT], selsq[:, kt * 16:(kt + 1) * 16],
                        sqk[:, k2 * SLT:(k2 + 1) * SLT], start=(kt == 0), stop=(kt == 3)),
                        reads=[nm + "sqk", nm + "selsq"], writes=["ps3"])
            rq = R[qname]
            p.op("dve", lambda h, rq=rq: h.tensor_scalar_add(rq, PS[3][0:16, 0:SLT], 1e-6),
                 reads=["ps3"], writes=[(nm + qname, sl)])
            p.op("act", lambda h, rq=rq: h.activation(rq, rq, AF.Ln),
                 reads=[(nm + qname, sl)], writes=[(nm + qname, sl)])
            p.op("dve", lambda h, rq=rq: h.tensor_scalar_mul(rq, rq, -0.5),
                 reads=[(nm + qname, sl)], writes=[(nm + qname, sl)])
        if s2gate("d", locals()):
            continue
        p.op("dve", lambda h: h.tensor_add(R["RA"], R["lg"], R["lnrk"]),
             reads=[(nm + "lg", sl), (nm + "lnrk", sl)], writes=[(nm + "RA", sl)])
        p.op("dve", lambda h: h.tensor_add(R["RA"], R["RA"], R["lnb"]),
             reads=[(nm + "RA", sl), (nm + "lnb", sl)], writes=[(nm + "RA", sl)])
        p.op("dve", lambda h: h.tensor_sub(R["CA"], R["lnrk"], R["lg"]),
             reads=[(nm + "lg", sl), (nm + "lnrk", sl)], writes=[(nm + "CA", sl)])
        p.op("dve", lambda h: h.tensor_add(R["RP"], R["lg"], R["lnrq"]),
             reads=[(nm + "lg", sl), (nm + "lnrq", sl)], writes=[(nm + "RP", sl)])
        p.op("dve", lambda h: h.tensor_scalar_add(R["RP"], R["RP"], -2.0794415416798357),
             reads=[(nm + "RP", sl)], writes=[(nm + "RP", sl)])
        p.op("dve", lambda h: h.tensor_sub(
            offs64.rearrange("p (c two) -> p c two", two=2),
            offs64.rearrange("p (c two) -> p c two", two=2),
            carry[:, 2:4].rearrange("p (c one) -> p c one", one=1).broadcast_to([16, 2, 2])),
            reads=[nm + "offs64", (nm + "carry", "o")], writes=[nm + "offs64"])
        p.op("dve", lambda h: h.memset(cb4[:, 0:1], 0.0), writes=[nm + "cb4"])
        p.op("dve", lambda h: h.memset(cb4[:, 2:3], 0.0), writes=[nm + "cb4"])
        p.op("dve", lambda h: h.tensor_copy(cb4[:, 1:2], offs64[:, 0:1]),
             reads=[nm + "offs64"], writes=[nm + "cb4"])
        p.op("dve", lambda h: h.tensor_copy(cb4[:, 3:4], offs64[:, 2:3]),
             reads=[nm + "offs64"], writes=[nm + "cb4"])
        p.op("dve", lambda h: h.tensor_sub(
            R["rt1"].rearrange("p (c t) -> p c t", t=64),
            R["lg"].rearrange("p (c t) -> p c t", t=64),
            cb4.rearrange("p (c one) -> p c one", one=1).broadcast_to([16, 4, 64])),
            reads=[(nm + "lg", sl), nm + "cb4"], writes=[(nm + "rt1", sl)])
        p.op("dve", lambda h: h.tensor_add(R["nabrn"], R["rt1"], R["lnrk"]),
             reads=[(nm + "rt1", sl), (nm + "lnrk", sl)], writes=[(nm + "nabrn", sl)])
        p.op("dve", lambda h: h.tensor_add(R["nabrn"], R["nabrn"], R["lnb"]),
             reads=[(nm + "nabrn", sl), (nm + "lnb", sl)], writes=[(nm + "nabrn", sl)])
        p.op("act", lambda h: h.activation(R["nabrn"], R["nabrn"], AF.Exp),
             reads=[(nm + "nabrn", sl)], writes=[(nm + "nabrn", sl)])
        p.op("dve", lambda h: h.tensor_scalar_mul(R["nabrn"], R["nabrn"], -1.0),
             reads=[(nm + "nabrn", sl)], writes=[(nm + "nabrn", sl)])
        p.op("dve", lambda h: h.tensor_add(R["aq"], R["rt1"], R["lnrq"]),
             reads=[(nm + "rt1", sl), (nm + "lnrq", sl)], writes=[(nm + "aq", sl)])
        p.op("dve", lambda h: h.tensor_scalar_add(R["aq"], R["aq"], -2.0794415416798357),
             reads=[(nm + "aq", sl)], writes=[(nm + "aq", sl)])
        p.op("act", lambda h: h.activation(R["aq"], R["aq"], AF.Exp),
             reads=[(nm + "aq", sl)], writes=[(nm + "aq", sl)])
        p.op("dve", lambda h: h.tensor_sub(
            R["wbar"].rearrange("p (c t) -> p c t", t=64),
            offs64.rearrange("p (c one) -> p c one", one=1).broadcast_to([16, 4, 64]),
            R["lg"].rearrange("p (c t) -> p c t", t=64)),
            reads=[nm + "offs64", (nm + "lg", sl)], writes=[(nm + "wbar", sl)])
        p.op("dve", lambda h: h.tensor_add(R["wbar"], R["wbar"], R["lnrk"]),
             reads=[(nm + "wbar", sl), (nm + "lnrk", sl)], writes=[(nm + "wbar", sl)])
        p.op("act", lambda h: h.activation(R["wbar"], R["wbar"], AF.Exp),
             reads=[(nm + "wbar", sl)], writes=[(nm + "wbar", sl)])
        p.op("dve", lambda h: h.tensor_copy(
            R["afA"].rearrange("p (c t) -> p c t", t=128),
            carry[:, 4:8:2].rearrange("p (c one) -> p c one", one=1).broadcast_to([16, 2, 128])),
            reads=[(nm + "carry", "af")], writes=[(nm + "afA", sl)])
        p.op("dve", lambda h: h.tensor_copy(
            R["afB"].rearrange("p (c t) -> p c t", t=128),
            carry[:, 5:8:2].rearrange("p (c one) -> p c one", one=1).broadcast_to([16, 2, 128])),
            reads=[(nm + "carry", "af")], writes=[(nm + "afB", sl)])
        split_rows(nc, p, nm, R, rowbb, sl)
        if ST < 3:
            if os.environ.get("KZ_TAILCP"):
                attn_cp(nc, p, st, po, L, locals(), sl * 2, 0, sl)
            continue
        for tw in range(2):
            tt = sl * 2 + tw
            srcs = [R["CAh"], R["CAm"], R["CAl"], R["RPh"], R["RPm"], R["RPl"],
                    R["RA"], R["CA"], R["beta"], R["nabrn"], R["wbar"],
                    R["aq"], R["afA"], R["afB"]]
            keys = ["CAh", "CAm", "CAl", "RPh", "RPm", "RPl", "RA", "CA", "beta",
                    "nabrn", "wbar", "aq", "afA", "afB"]
            for qi in range(14):
                srcap = srcs[qi][:, tw * 128:(tw + 1) * 128]
                rk = (nm + keys[qi], sl)
                p.op("pe", lambda h, srcap=srcap, qi=qi: h.transpose(
                    PS[4][:, qi * 16:(qi + 1) * 16], srcap, st["identf"][0:16, 0:16]),
                    reads=[rk, "c_identf"], writes=[("ps4", qi)], label=f"{nm}sct{tt}_{qi}")
            p.op("dve", lambda h, tw=tw: h.tensor_copy(
                scolb[:, tw * 96:(tw + 1) * 96], PS[4][:, 0:96]),
                reads=[("ps4", None)], writes=[(nm + "scolb", tw)])
            p.op("dve", lambda h, tw=tw: h.tensor_copy(
                scolf[:, tw * 128:(tw + 1) * 128], PS[4][:, 96:224]),
                reads=[("ps4", None)], writes=[(nm + "scolf", tw)])
        if ST < 4:
            if os.environ.get("KZ_TAILCP"):
                attn_cp(nc, p, st, po, L, locals(), sl * 2, 0, sl)
            continue
        for tw in range(2):
            for j, ct in enumerate(range(4, 8)):
                p.op("pe", lambda h, ct=ct, tw=tw, j=j: h.transpose(
                    psb[:, (j % 8) * 128:(j % 8 + 1) * 128],
                    qkvC[:, ct * SLT + tw * 128: ct * SLT + (tw + 1) * 128], ident[:]),
                    reads=[(nm + "qkvC", ct), "c_ident"], writes=[("psb", j)],
                    label=f"{nm}ktr{sl}{tw}{j}")
            p.op("dve", lambda h, tw=tw: h.tensor_copy(
                k_tm[:, tw * 512:(tw + 1) * 512], psb[:, 0:512]),
                reads=[("psb", None)], writes=[(nm + "k_tm", tw)])
            for j, ct in enumerate(range(8, 16)):
                p.op("pe", lambda h, ct=ct, tw=tw, j=j: h.transpose(
                    psb[:, j * 128:(j + 1) * 128],
                    qkvC[:, ct * SLT + tw * 128: ct * SLT + (tw + 1) * 128], ident[:]),
                    reads=[(nm + "qkvC", ct), "c_ident"], writes=[("psb", j)],
                    label=f"{nm}vtr{sl}{tw}{j}")
            for s in range(16):
                p.op("dve", lambda h, tw=tw, s=s: h.tensor_scalar(
                    vbeta[:, tw * 1024 + s * 64: tw * 1024 + (s + 1) * 64],
                    psb[:, s * 64:(s + 1) * 64],
                    scolf[:, scol_f(tw, 2, s):scol_f(tw, 2, s) + 1], None, ALU.mult),
                    reads=[("psb", None), (nm + "scolf", tw)],
                    writes=[(nm + "vbeta", (tw, s))])
        if L.get("dbg"):
            p.dma_op(L["sdb"], L["dbg"]["dbgs"][:, sl * 480: sl * 480 + 96].bitcast(BF16),
                     scolb[:, :], reads=[(nm + "scolb", None)])
            p.dma_op(L["sdb"], L["dbg"]["dbgs"][:, sl * 480 + 96: sl * 480 + 352],
                     scolf[:, :], reads=[(nm + "scolf", None)])
        if ST < 5:
            if os.environ.get("KZ_TAILCP"):
                attn_cp(nc, p, st, po, L, locals(), sl * 2, 0, sl)
            continue
        for tw in range(2):
            cp = sl * 2 + tw
            attn_cp(nc, p, st, po, L, locals(), cp, tw, sl)
        if ST < 6:
            continue
        wout_slab(nc, p, st, po, L, locals(), sl, xin_ap, xout_ap)


def split_rows(nc, p, nm, R, bb, sl):
    for src in ("CA", "RP"):
        for lv, name in enumerate(("h", "m", "l")):
            dkey = (nm + src + name, sl)
            rem = R[src] if lv == 0 else R["spf"]
            rkey = (nm + src, sl) if lv == 0 else (nm + "spf", sl)
            p.op("dve", lambda h, rem=rem: h.tensor_copy(bb, rem),
                 reads=[rkey], writes=[(nm + "bb", sl)])
            p.op("dve", lambda h, dst=R[src + name]: h.tensor_copy(dst, bb),
                 reads=[(nm + "bb", sl)], writes=[dkey])
            if lv < 2:
                p.op("dve", lambda h, rem=rem, dst=R[src + name]: h.tensor_sub(
                    R["spf"], rem, dst),
                    reads=[rkey, dkey], writes=[(nm + "spf", sl)])


def attn_cp(nc, p, st, po, L, S, cp, tw, sl):
    nm = "at"
    PS, psb = st["ps"], st["psb"]
    ident = st["ident"]
    wpool = L["wpool"]
    WZ = L["WZ"]
    selsA, selsP = L["selsA"], L["selsP"]
    maskA, maskP, nwrep = L["maskA"], L["maskP"], L["nwrep"]
    qkvC, scolb, scolf, tile6 = S["qkvC"], S["scolb"], S["scolf"], S["tile6"]
    dA, dP, Ebuf, Mt, Nt, Tt, Pt = S["dA"], S["dP"], S["Ebuf"], S["Mt"], S["Nt"], S["Tt"], S["Pt"]
    ch = S["ch"]
    k_tm, vbeta = S["k_tm"], S["vbeta"]
    szb, r0b, ub, uwb, ost, onb, osq, rsq = S["szb"], S["r0b"], S["ub"], S["uwb"], S["ost"], S["onb"], S["osq"], S["rsq"]
    afrep, Sf, Sb = S["afrep"], S["Sf"], S["Sb"]
    xsl2 = S["xsl2"]
    scol_b, scol_f = S["scol_b"], S["scol_f"]
    ccol = tw * 128
    import os
    CP = int(os.environ.get("KZ_CP", "9"))
    GS = int(os.environ.get("KZ_GS", "9"))

    T6M = os.environ.get("KZ_T6", "on")
    if T6M == "pre":
        for g in range(4):
            sls = range(g * 4, g * 4 + 4)
            for i, s in enumerate(sls):
                inap = scolb[:, scol_b(tw, 0, s): scol_b(tw, 0, s) + 81:16]
                p.op("pe", lambda h, inap=inap, i=i: h.transpose(
                    psb[:, i * 128:(i + 1) * 128][0:6, :], inap, ident[:]),
                    reads=[(nm + "scolb", tw), "c_ident"], writes=[("psb", i)],
                    label=f"{nm}t6_{cp}_{s}")
            p.op("dve", lambda h: h.tensor_copy(tile6[0:6, :], psb[0:6, 0:512]),
                 reads=[("psb", None)], writes=[nm + "tile6"])
    for g in range(4):
        sls = range(g * 4, g * 4 + 4)
        if T6M == "on":
            for i, s in enumerate(sls):
                inap = scolb[:, scol_b(tw, 0, s): scol_b(tw, 0, s) + 81:16]
                p.op("pe", lambda h, inap=inap, i=i: h.transpose(
                    psb[:, i * 128:(i + 1) * 128][0:6, :], inap, ident[:]),
                    reads=[(nm + "scolb", tw), "c_ident"], writes=[("psb", i)],
                    label=f"{nm}t6_{cp}_{s}")
            p.op("dve", lambda h: h.tensor_copy(tile6[0:6, :], psb[0:6, 0:512]),
                 reads=[("psb", None)], writes=[nm + "tile6"])
        if GS < 2:
            continue
        EAx = int(os.environ.get("KZ_EA", "9"))
        KMM = os.environ.get("KZ_MM", "sel")
        if KMM == "none":
            pass
        elif KMM.startswith("bank"):
            bno = int(KMM[4:])
            for i, s in enumerate(sls):
                p.op("pe", lambda h, i=i, bno=bno: h.matmul(
                    PS[bno][:, i * 128:(i + 1) * 128], selsA[0:6, :], tile6[0:6, i * 128:(i + 1) * 128],
                    start=True, stop=True),
                    reads=[nm + "tile6", nm + "selA"], writes=[(f"ps{bno}", i)],
                    label=f"{nm}eA{cp}_{s}")
            for i, s in enumerate(sls):
                if g * 4 + i >= 1:
                    continue
                p.op("dve", lambda h, i=i, bno=bno: h.tensor_copy(
                    Ebuf[:, i * 128:(i + 1) * 128], PS[bno][:, i * 128:(i + 1) * 128]),
                    reads=[(f"ps{bno}", i)], writes=[(nm + "Ebuf", i)],
                    label=f"{nm}eAs{cp}_{s}")
            continue
        elif KMM == "plain":
            for i, s in enumerate(sls):
                p.op("pe", lambda h, i=i: h.matmul(
                    PS[2][:, i * 128:(i + 1) * 128], xsl2[:, 0:128], xsl2[:, 0:128],
                    start=True, stop=True),
                    reads=["gxb"], writes=[("ps2", i)],
                    label=f"{nm}eA{cp}_{s}")
        else:
            for i, s in enumerate(sls):
                p.op("pe", lambda h, i=i: h.matmul(
                    PS[2][:, i * 128:(i + 1) * 128], selsA[0:6, :], tile6[0:6, i * 128:(i + 1) * 128],
                    start=True, stop=True),
                    reads=[nm + "tile6", nm + "selA"], writes=[("ps2", i)],
                    label=f"{nm}eA{cp}_{s}")
        if EAx < 2:
            continue
        NST = int(os.environ.get("KZ_NST", "99"))
        VK = os.environ.get("KZ_VK", "stt")
        for i, s in enumerate(sls):
            if g * 4 + i >= NST:
                continue
            if VK == "stt":
                p.op("dve", lambda h, s=s, i=i: h.scalar_tensor_tensor(
                    Ebuf[:, i * 128:(i + 1) * 128], PS[2][:, i * 128:(i + 1) * 128],
                    scolf[:, scol_f(tw, 0, s):scol_f(tw, 0, s) + 1], maskA, ALU.add, ALU.add),
                    reads=[("ps2", i), (nm + "scolf", tw), nm + "maskA"], writes=[(nm + "Ebuf", i)],
                    label=f"{nm}eAs{cp}_{s}")
            elif VK == "tt":
                p.op("dve", lambda h, s=s, i=i: h.tensor_tensor(
                    Ebuf[:, i * 128:(i + 1) * 128], PS[2][:, i * 128:(i + 1) * 128],
                    maskA, ALU.add),
                    reads=[("ps2", i), nm + "maskA"], writes=[(nm + "Ebuf", i)],
                    label=f"{nm}eAs{cp}_{s}")
            elif VK == "ts":
                p.op("dve", lambda h, s=s, i=i: h.tensor_scalar(
                    Ebuf[:, i * 128:(i + 1) * 128], PS[2][:, i * 128:(i + 1) * 128],
                    scolf[:, scol_f(tw, 0, s):scol_f(tw, 0, s) + 1], None, ALU.add),
                    reads=[("ps2", i), (nm + "scolf", tw)], writes=[(nm + "Ebuf", i)],
                    label=f"{nm}eAs{cp}_{s}")
            elif VK == "cp":
                p.op("dve", lambda h, s=s, i=i: h.tensor_copy(
                    Ebuf[:, i * 128:(i + 1) * 128], PS[2][:, i * 128:(i + 1) * 128]),
                    reads=[("ps2", i)], writes=[(nm + "Ebuf", i)],
                    label=f"{nm}eAs{cp}_{s}")
            elif VK == "act":
                p.op("act", lambda h, s=s, i=i: h.activation(
                    dA[:, i * 128:(i + 1) * 128], PS[2][:, i * 128:(i + 1) * 128], AF.Exp),
                    reads=[("ps2", i)], writes=[nm + "dA"],
                    label=f"{nm}eAs{cp}_{s}")
            elif VK == "cpnosync":
                p.op("dve", lambda h, s=s, i=i: h.tensor_copy(
                    Ebuf[:, i * 128:(i + 1) * 128], PS[2][:, i * 128:(i + 1) * 128]),
                    reads=[], writes=[(nm + "Ebuf", i)],
                    label=f"{nm}eAs{cp}_{s}")
            elif VK == "cpb":
                p.op("dve", lambda h, s=s, i=i: h.tensor_copy(
                    dA[:, i * 128:(i + 1) * 128], PS[2][:, i * 128:(i + 1) * 128]),
                    reads=[("ps2", i)], writes=[nm + "dA"],
                    label=f"{nm}eAs{cp}_{s}")
        if EAx < 3:
            continue
        p.op("act", lambda h: h.activation(dA[:, :], Ebuf[:, :], AF.Exp),
             reads=[(nm + "Ebuf", None)], writes=[nm + "dA"], label=f"{nm}expA{cp}{g}")
        if GS < 3:
            continue
        for i, s in enumerate(sls):
            kb, kt_ = kbase(s), ktile(s)
            kap = qkvC[kb:kb + 64, kt_ * SLT + ccol: kt_ * SLT + ccol + 128]
            p.op("pe", lambda h, kap=kap, i=i, kb=kb: h.matmul(
                PS[2][:, i * 128:(i + 1) * 128], kap, kap, start=True, stop=True,
                tile_position=(kb, 0)),
                reads=[(nm + "qkvC", None)], writes=[("ps2", i)], label=f"{nm}gkk{cp}_{s}")
        p.op("dve", lambda h: h.tensor_mul(Mt[:, :], PS[2][:, 0:512], dA[:, :]),
            reads=[("ps2", None), nm + "dA"], writes=[(nm + "Mt", g)], label=f"{nm}mkM{cp}{g}")
        if GS < 4:
            continue
        for i, s in enumerate(sls):
            p.op("pe", lambda h, i=i: h.transpose(
                psb[:, i * 128:(i + 1) * 128], Mt[:, i * 128:(i + 1) * 128], ident[:]),
                reads=[(nm + "Mt", g), "c_ident"], writes=[("psb", i)],
                label=f"{nm}n0{cp}_{s}")
        p.op("dve", lambda h: h.tensor_copy(Nt[:, :], psb[:, 0:512]),
             reads=[("psb", None)], writes=[(nm + "Nt", g)])
        if GS < 5:
            continue
        for i, s in enumerate(sls):
            p.op("pe", lambda h, i=i: h.matmul(
                PS[2][:, i * 128:(i + 1) * 128], selsP[0:6, :], tile6[0:6, i * 128:(i + 1) * 128],
                start=True, stop=True),
                reads=[nm + "tile6", nm + "selP"], writes=[("ps2", i)],
                label=f"{nm}eP{cp}_{s}")
        for i, s in enumerate(sls):
            p.op("dve", lambda h, s=s, i=i: h.scalar_tensor_tensor(
                Ebuf[:, i * 128:(i + 1) * 128], PS[2][:, i * 128:(i + 1) * 128],
                scolf[:, scol_f(tw, 1, s):scol_f(tw, 1, s) + 1], maskP, ALU.add, ALU.add),
                reads=[("ps2", i), (nm + "scolf", tw), nm + "maskP"], writes=[(nm + "Ebuf", i)],
                label=f"{nm}ePs{cp}_{s}")
        p.op("act", lambda h: h.activation(dP[:, :], Ebuf[:, :], AF.Exp),
             reads=[(nm + "Ebuf", None)], writes=[nm + "dP"], label=f"{nm}expP{cp}{g}")
        for i, s in enumerate(sls):
            kb, kt_, qt_ = kbase(s), ktile(s), qtile(s)
            kap = qkvC[kb:kb + 64, kt_ * SLT + ccol: kt_ * SLT + ccol + 128]
            qap = qkvC[kb:kb + 64, qt_ * SLT + ccol: qt_ * SLT + ccol + 128]
            p.op("pe", lambda h, kap=kap, qap=qap, i=i, kb=kb: h.matmul(
                PS[2][:, i * 128:(i + 1) * 128], kap, qap, start=True, stop=True,
                tile_position=(kb, 0)),
                reads=[(nm + "qkvC", None)], writes=[("ps2", i)], label=f"{nm}gkq{cp}_{s}")
        p.op("dve", lambda h, g=g: h.tensor_mul(
            Pt[:, g * 512:(g + 1) * 512], PS[2][:, 0:512], dP[:, :]),
            reads=[("ps2", None), nm + "dP"], writes=[(nm + "Pt", g)], label=f"{nm}mkP{cp}{g}")
        if CP >= 2:
            chain_group(nc, p, st, po, L, S, cp, g)

    if CP < 3:
        return
    for half in range(2):
        for a in range(NA):
            p.op("pe", lambda h, a=a, half=half: h.matmul(
                PS[5][:, 0:512], xsl2[:, a * SLT + ccol: a * SLT + ccol + 128],
                wpool[:, WZ + a * 1024 + half * 512: WZ + a * 1024 + (half + 1) * 512],
                start=(a == 0), stop=(a == NA - 1)),
                reads=[("wpool", (nm, "z", a)), "gxb"], writes=["ps5"],
                label=f"{nm}z{cp}_{half}{a}")
        p.op("act", lambda h, half=half: h.activation(
            szb[:, half * 512:(half + 1) * 512], PS[5][:, 0:512], AF.Silu),
            reads=["ps5"], writes=[(nm + "szb", half)], label=f"{nm}zs{cp}{half}")
    p.op("dve", lambda h: h.tensor_mul(
        szb[:].rearrange("p (s c) -> p s c", c=64), szb[:].rearrange("p (s c) -> p s c", c=64),
        nwrep.rearrange("p (one c) -> p one c", one=1).broadcast_to([128, 16, 64])),
        reads=[(nm + "szb", None), nm + "nwrep"], writes=[(nm + "szb", None)],
        label=f"{nm}znw{cp}")

    if CP < 4:
        return
    for c01 in range(2):
        attn_chunk(nc, p, st, po, L, S, cp, tw, c01)

    if CP < 5:
        return
    for bi, bk, bank in ((0, "ps6", PS[6]), (1, "ps5", PS[5])):
        p.op("dve", lambda h, bi=bi, bank=bank: h.tensor_add(
            ost[:, bi * 512:(bi + 1) * 512], ost[:, bi * 512:(bi + 1) * 512],
            bank[:, 0:512]),
            reads=[(bk, None), (nm + "ost", None)], writes=[(nm + "ost", None)],
            label=f"{nm}oev{cp}{bi}")

    if L.get("dbg"):
        p.dma_op(L["sdb"], L["dbg"]["dbgo"][:, cp * 1024:(cp + 1) * 1024], ost[:, :],
                 reads=[(nm + "ost", None)])
        if cp == 0:
            p.dma_op(L["sdb"], L["dbg"]["dbgt"][:, 0:2048], S["Tt"][:, :], reads=[(nm + "Tt", None)])
            p.dma_op(L["sdb"], L["dbg"]["dbgt"][:, 2048:4096], S["Pt"][:, :], reads=[(nm + "Pt", None)])
            p.dma_op(L["sdb"], L["dbg"]["dbgt"][:, 4096:5120], S["ub"][:, :], reads=[(nm + "ub", None)])
            p.dma_op(L["sdb"], L["dbg"]["dbgt"][:, 5120:5632], S["Sb"][:, :], reads=[nm + "Sb"])
            p.dma_op(L["sdb"], L["dbg"]["dbgt"][:, 5632:6656], S["uwb"][:, :], reads=[nm + "scr2"])
    p.op("dve", lambda h: h.tensor_mul(osq[:, :], ost[:, :], ost[:, :]),
         reads=[(nm + "ost", None)], writes=[nm + "oscr"], label=f"{nm}osq{cp}")
    p.op("dve", lambda h: h.tensor_reduce(
        rsq, osq[:].rearrange("p (s c) -> p s c", c=64), mybir.AxisListType.X, ALU.add),
        reads=[nm + "oscr"], writes=[nm + "rsq"], label=f"{nm}ovar{cp}")
    p.op("dve", lambda h: h.tensor_scalar(rsq, rsq, 1.0 / 64, 1e-6, ALU.mult, ALU.add),
         reads=[nm + "rsq"], writes=[nm + "rsq"])
    p.op("dve", lambda h: h.reciprocal(rsq, rsq), reads=[nm + "rsq"], writes=[nm + "rsq"])
    p.op("act", lambda h: h.activation(rsq, rsq, AF.Sqrt), reads=[nm + "rsq"], writes=[nm + "rsq"])
    p.op("dve", lambda h: h.tensor_mul(
        onb[:].rearrange("p (s c) -> p s c", c=64), ost[:].rearrange("p (s c) -> p s c", c=64),
        rsq.rearrange("p (s one) -> p s one", one=1).broadcast_to([128, 16, 64])),
        reads=[(nm + "ost", None), nm + "rsq"], writes=[nm + "oscr"], label=f"{nm}on1{cp}")
    p.op("dve", lambda h: h.tensor_mul(onb[:, :], onb[:, :], szb[:, :]),
         reads=[nm + "oscr", (nm + "szb", None)], writes=[nm + "oscr"], label=f"{nm}on2{cp}")
    onbb = S["onbb"]
    p.op("dve", lambda h: h.tensor_copy(onbb[:, :], onb[:, :]),
         reads=[nm + "oscr"], writes=[nm + "scr2"], label=f"{nm}oncast{cp}")
    if L.get("dbg"):
        p.dma_op(L["sdb"], L["dbg"]["dbgn"][:, cp * 1024:(cp + 1) * 1024], onbb[:, :],
                 reads=[nm + "scr2"])
    ontp = S["ontp"]
    for j in range(8):
        p.op("pe", lambda h, j=j: h.transpose(
            psb[:, j * 128:(j + 1) * 128], onbb[:, j * 128:(j + 1) * 128], ident[:]),
            reads=[nm + "scr2", "c_ident"], writes=[("psb", j)], label=f"{nm}ontr{cp}{j}")
    p.op("dve", lambda h, tw=tw: h.tensor_copy(
        ontp[:].rearrange("p (v t) -> p v t", t=SLT)[:, :, tw * 128:(tw + 1) * 128],
        psb[:, 0:1024].rearrange("p (v t) -> p v t", t=128)),
        reads=[("psb", None)], writes=[(nm + "ontp", tw)], label=f"{nm}onev{cp}")


def chain_group(nc, p, st, po, L, S, cp, g):
    nm = "at"
    PS = st["ps"]
    ident = st["ident"]
    Mt, Nt, Tt, ch = S["Mt"], S["Nt"], S["Tt"], S["ch"]
    p.op("dve", lambda h: h.scalar_tensor_tensor(
        ch["Pc"][:, :].rearrange("p (s c) -> p s c", c=128),
        Nt[:, :].rearrange("p (s c) -> p s c", c=128), -1.0,
        ident[:].rearrange("p (one c) -> p one c", one=1).broadcast_to([128, 4, 128]),
        ALU.mult, ALU.add),
        reads=[(nm + "Nt", g), "c_ident"], writes=[nm + "chPc"],
        label=f"{nm}Pinit{cp}{g}")
    Mcur_t, Mcur_k = Mt, (nm + "Mt", g)
    Ncur_t, Ncur_k = Nt, (nm + "Nt", g)
    Pcur_t, Pcur_k = ch["Pc"], nm + "chPc"
    Mn, Mnk = ch["Mc"], nm + "chMc"
    Nn, Nnk = ch["Nc"], nm + "chNc"
    Pn, Pnk = ch["Pn"], nm + "chPn"
    Mfree, Mfk = ch["Mn"], nm + "chMn"
    Nfree, Nfk = ch["Nn"], nm + "chNn"
    Pfree, Pfk = ch["Pc"], nm + "chPc"
    for r in range(6):
        do_m = r <= 4
        do_n = r <= 3
        do_p = r >= 1
        for i in range(4):
            if do_m:
                p.op("pe", lambda h, i=i, N=Ncur_t, M=Mcur_t: h.matmul(
                    PS[3][:, i * 128:(i + 1) * 128], N[:, i * 128:(i + 1) * 128],
                    M[:, i * 128:(i + 1) * 128], start=True, stop=True),
                    reads=[Mcur_k, Ncur_k], writes=[("ps3", i)], label=f"{nm}cM{cp}{g}{r}{i}")
            if do_n:
                p.op("pe", lambda h, i=i, N=Ncur_t, M=Mcur_t: h.matmul(
                    PS[4][:, i * 128:(i + 1) * 128], M[:, i * 128:(i + 1) * 128],
                    N[:, i * 128:(i + 1) * 128], start=True, stop=True),
                    reads=[Mcur_k, Ncur_k], writes=[("ps4", i)], label=f"{nm}cN{cp}{g}{r}{i}")
            if do_p:
                p.op("pe", lambda h, i=i, M=Mcur_t, P=Pcur_t: h.matmul(
                    PS[2][:, i * 128:(i + 1) * 128], M[:, i * 128:(i + 1) * 128],
                    P[:, i * 128:(i + 1) * 128], start=True, stop=True),
                    reads=[Mcur_k, Pcur_k], writes=[("ps2", i)], label=f"{nm}cP{cp}{g}{r}{i}")
        if do_m:
            p.op("dve", lambda h, Mn=Mn: h.tensor_copy(Mn[:, :], PS[3][:, 0:512]),
                 reads=[("ps3", None)], writes=[Mnk], label=f"{nm}cMe{cp}{g}{r}")
            Mcur_t, Mcur_k, Mn, Mnk, Mfree, Mfk = Mn, Mnk, Mfree, Mfk, Mn, Mnk
        if do_n:
            p.op("dve", lambda h, Nn=Nn: h.tensor_copy(Nn[:, :], PS[4][:, 0:512]),
                 reads=[("ps4", None)], writes=[Nnk], label=f"{nm}cNe{cp}{g}{r}")
            Ncur_t, Ncur_k, Nn, Nnk, Nfree, Nfk = Nn, Nnk, Nfree, Nfk, Nn, Nnk
        if do_p:
            if r < 5:
                p.op("dve", lambda h, Pn=Pn, P=Pcur_t: h.tensor_add(
                    Pn[:, :], PS[2][:, 0:512], P[:, :]),
                    reads=[("ps2", None), Pcur_k], writes=[Pnk], label=f"{nm}cPe{cp}{g}{r}")
                Pcur_t, Pcur_k, Pn, Pnk, Pfree, Pfk = Pn, Pnk, Pfree, Pfk, Pn, Pnk
            else:
                p.op("dve", lambda h, P=Pcur_t, g=g: h.tensor_add(
                    Tt[:, g * 512:(g + 1) * 512], PS[2][:, 0:512], P[:, :]),
                    reads=[("ps2", None), Pcur_k], writes=[(nm + "Tt", g)],
                    label=f"{nm}cTt{cp}{g}")


def attn_chunk(nc, p, st, po, L, S, cp, tw, c01):
    nm = "at"
    PS = st["ps"]
    qkvC, k_tm, vbeta = S["qkvC"], S["k_tm"], S["vbeta"]
    scolb, scolf = S["scolb"], S["scolf"]
    Tt = S["Tt"]
    r0b, ub, uwb = S["r0b"], S["ub"], S["uwb"]
    afrep, Sf, Sb = S["afrep"], S["Sf"], S["Sb"]
    scol_b, scol_f = S["scol_b"], S["scol_f"]
    par = 64 * c01
    ccol = tw * 128 + 64 * c01
    for s in range(16):
        kb = kbase(s)
        bank, bk = (PS[2], "ps2") if s < 8 else (PS[3], "ps3")
        kap = qkvC[kb:kb + 64, ktile(s) * SLT + ccol: ktile(s) * SLT + ccol + 64]
        p.op("pe", lambda h, kap=kap, s=s, kb=kb, bank=bank, par=par: h.matmul(
            bank[par:par + 64, (s % 8) * 64:(s % 8 + 1) * 64], kap,
            Sb[kb:kb + 64, (s % 8) * 64:(s % 8 + 1) * 64],
            start=True, stop=True, tile_position=(kb, par)),
            reads=[(nm + "qkvC", None), nm + "Sb"], writes=[(bk, s % 8)],
            label=f"{nm}B1_{cp}{c01}_{s}")
    for bi, bank, bk in ((0, PS[2], "ps2"), (1, PS[3], "ps3")):
        p.op("dve", lambda h, bi=bi, bank=bank, par=par: h.tensor_mul(
            r0b[par:par + 64, bi * 512:(bi + 1) * 512].rearrange("p (s c) -> p s c", c=64),
            bank[par:par + 64, 0:512].rearrange("p (s c) -> p s c", c=64),
            scolf[par:par + 64, scol_f(tw, 3, 8 * bi):scol_f(tw, 3, 8 * bi) + 8].rearrange(
                "p (s one) -> p s one", one=1).broadcast_to([64, 8, 64])),
            reads=[(bk, None), (nm + "scolf", tw)], writes=[nm + "scr1"],
            label=f"{nm}r0a{cp}{c01}{bi}")
        p.op("dve", lambda h, bi=bi, par=par, tw=tw: h.tensor_add(
            r0b[par:par + 64, bi * 512:(bi + 1) * 512],
            r0b[par:par + 64, bi * 512:(bi + 1) * 512],
            vbeta[par:par + 64, tw * 1024 + bi * 512: tw * 1024 + (bi + 1) * 512]),
            reads=[(nm + "r0", (c01, bi)), (nm + "vbeta", None)],
            writes=[nm + "scr1"], label=f"{nm}r0b{cp}{c01}{bi}")
    for s in range(16):
        bank, bk = (PS[2], "ps2") if s < 8 else (PS[3], "ps3")
        p.op("pe", lambda h, s=s, bank=bank, par=par: h.matmul(
            bank[par:par + 64, (s % 8) * 64:(s % 8 + 1) * 64],
            Tt[par:par + 64, s * 128 + par: s * 128 + par + 64],
            r0b[par:par + 64, (s // 8) * 512 + (s % 8) * 64: (s // 8) * 512 + (s % 8 + 1) * 64],
            start=True, stop=True, tile_position=(par, par)),
            reads=[(nm + "Tt", s // 4), nm + "scr1"],
            writes=[(bk, s % 8)], label=f"{nm}B3_{cp}{c01}_{s}")
    for bi, bank, bk in ((0, PS[2], "ps2"), (1, PS[3], "ps3")):
        p.op("dve", lambda h, bi=bi, bank=bank, par=par: h.tensor_copy(
            ub[par:par + 64, bi * 512:(bi + 1) * 512], bank[par:par + 64, 0:512]),
            reads=[(bk, None)], writes=[(nm + "ub", (c01, bi))], label=f"{nm}uev{cp}{c01}{bi}")
        p.op("dve", lambda h, bi=bi, bank=bank, par=par, tw=tw: h.tensor_mul(
            uwb[par:par + 64, bi * 512:(bi + 1) * 512].rearrange("p (s c) -> p s c", c=64),
            bank[par:par + 64, 0:512].rearrange("p (s c) -> p s c", c=64),
            scolf[par:par + 64, scol_f(tw, 4, 8 * bi):scol_f(tw, 4, 8 * bi) + 8].rearrange(
                "p (s one) -> p s one", one=1).broadcast_to([64, 8, 64])),
            reads=[(bk, None), (nm + "scolf", tw)], writes=[nm + "scr2"],
            label=f"{nm}uw{cp}{c01}{bi}")
    ost = S["ost"]
    for s in range(16):
        kb = kbase(s)
        bank, bk = (PS[2], "ps2") if s < 8 else (PS[3], "ps3")
        qap = qkvC[kb:kb + 64, qtile(s) * SLT + ccol: qtile(s) * SLT + ccol + 64]
        p.op("pe", lambda h, qap=qap, s=s, kb=kb, bank=bank, par=par: h.matmul(
            bank[par:par + 64, (s % 8) * 64:(s % 8 + 1) * 64], qap,
            Sb[kb:kb + 64, (s % 8) * 64:(s % 8 + 1) * 64],
            start=True, stop=True, tile_position=(kb, par)),
            reads=[(nm + "qkvC", None), nm + "Sb"], writes=[(bk, s % 8)],
            label=f"{nm}ocr{cp}{c01}_{s}")
    for bi, bank, bk in ((0, PS[2], "ps2"), (1, PS[3], "ps3")):
        p.op("dve", lambda h, bi=bi, bank=bank, par=par, tw=tw: h.tensor_mul(
            ost[par:par + 64, bi * 512:(bi + 1) * 512].rearrange("p (s c) -> p s c", c=64),
            bank[par:par + 64, 0:512].rearrange("p (s c) -> p s c", c=64),
            scolf[par:par + 64, scol_f(tw, 5, 8 * bi):scol_f(tw, 5, 8 * bi) + 8].rearrange(
                "p (s one) -> p s one", one=1).broadcast_to([64, 8, 64])),
            reads=[(bk, None), (nm + "scolf", tw)], writes=[(nm + "ost", (c01, bi))],
            label=f"{nm}ocev{cp}{c01}{bi}")
    for s in range(16):
        kb = kbase(s)
        p.op("pe", lambda h, s=s, kb=kb, par=par, tw=tw: h.matmul(
            PS[4][kb:kb + 64, (s % 8) * 64:(s % 8 + 1) * 64],
            k_tm[par:par + 64, tw * 512 + kh_of(s) * 64: tw * 512 + (kh_of(s) + 1) * 64],
            uwb[par:par + 64, (s // 8) * 512 + (s % 8) * 64: (s // 8) * 512 + (s % 8 + 1) * 64],
            start=True, stop=True, tile_position=(par, kb)),
            reads=[(nm + "k_tm", tw), nm + "scr2"], writes=[("ps4", "S")],
            label=f"{nm}B5_{cp}{c01}_{s}")
    afq = 6 + c01
    for half, base in ((0, 0), (1, 64)):
        p.op("dve", lambda h, half=half, base=base, tw=tw, afq=afq: h.tensor_copy(
            afrep[base:base + 64, 0:512].rearrange("p (s c) -> p s c", c=64),
            scolf[base:base + 64, scol_f(tw, afq, 8 * half):scol_f(tw, afq, 8 * half) + 8].rearrange(
                "p (s one) -> p s one", one=1).broadcast_to([64, 8, 64])),
            reads=[(nm + "scolf", tw)], writes=[(nm + "afrep", half)],
            label=f"{nm}afr{cp}{c01}{half}")
    p.op("dve", lambda h: h.tensor_mul(Sf[:, :], Sf[:, :], afrep[:, :]),
         reads=[nm + "Sf", (nm + "afrep", None)], writes=[nm + "Sf"],
         label=f"{nm}Sdk{cp}{c01}")
    p.op("dve", lambda h: h.tensor_add(Sf[:, :], Sf[:, :], PS[4][:, 0:512]),
         reads=[nm + "Sf", ("ps4", None)], writes=[nm + "Sf"], label=f"{nm}Sup{cp}{c01}")
    p.op("dve", lambda h: h.tensor_copy(Sb[:, :], Sf[:, :]),
         reads=[nm + "Sf"], writes=[nm + "Sb"], label=f"{nm}Smr{cp}{c01}")
    Tt_ = S["Tt"]
    Pt = S["Pt"]
    ub_ = S["ub"]
    for s in range(16):
        bank, bk = (PS[6], "ps6") if s < 8 else (PS[5], "ps5")
        p.op("pe", lambda h, s=s, bank=bank, par=par, c01=c01: h.matmul(
            bank[par:par + 64, (s % 8) * 64:(s % 8 + 1) * 64],
            Pt[par:par + 64, s * 128 + par: s * 128 + par + 64],
            ub_[par:par + 64, s * 64:(s + 1) * 64],
            start=True, stop=True,
            tile_position=(par, par), skip_group_check=True),
            reads=[(nm + "Pt", s // 4), (nm + "ub", None)], writes=[(bk, s % 8)],
            label=f"{nm}olo{cp}{c01}_{s}")


def wout_slab(nc, p, st, po, L, S, sl, xin_ap, xout_ap):
    nm = "at"
    PS = st["ps"]
    wouth = st["wouth"]
    gxf = st["gxf"]
    ontp = S["ontp"]
    sx, so, sxr = L["sx"], L["so"], L["sxr"]
    s0 = sl * SLT
    p.dma_op(sxr, gxf[:, 0:NA * SLT].rearrange("p (a t) -> p a t", a=NA),
             xin_ap[:, s0:s0 + SLT].rearrange("(a q) t -> q a t", q=128),
             writes=["gxf"], label=f"{nm}xres{sl}")
    for dt_ in range(NA):
        bank, bcol = PS[dt_ // 2], (dt_ % 2) * 256
        for kt in range(8):
            p.op("pe", lambda h, dt_=dt_, kt=kt, bank=bank, bcol=bcol: h.matmul(
                bank[:, bcol:bcol + SLT], wouth[:, kt * 1024 + dt_ * 128: kt * 1024 + (dt_ + 1) * 128],
                ontp[:, kt * SLT:(kt + 1) * SLT], start=(kt == 0), stop=(kt == 7)),
                reads=[("wpool", (nm, "out", kt)), (nm + "ontp", None)],
                writes=[(f"ps{dt_ // 2}", ("w", dt_ % 2))], label=f"{nm}WO{sl}_{dt_}{kt}")
        p.op("dve", lambda h, dt_=dt_, bank=bank, bcol=bcol: h.tensor_add(
            gxf[:, dt_ * SLT:(dt_ + 1) * SLT], gxf[:, dt_ * SLT:(dt_ + 1) * SLT],
            bank[:, bcol:bcol + SLT]),
            reads=[(f"ps{dt_ // 2}", None), "gxf"], writes=["gxf"],
            label=f"{nm}WOe{sl}_{dt_}")
    p.dma_op(so, xout_ap[:, s0:s0 + SLT].rearrange("(a q) t -> q a t", q=128),
             gxf[:, 0:NA * SLT].rearrange("p (a t) -> p a t", a=NA),
             reads=["gxf"], label=f"{nm}x2o{sl}")


def extend_shared(nc, st):
    st["gxf"] = nc.alloc_sbuf_tensor("gxf", [128, 4096], F32)
    st["gxb"] = nc.alloc_sbuf_tensor("gxb", [128, 4096], BF16)
    st["gxc"] = nc.alloc_sbuf_tensor("gxc", [128, 512], F32)
    st["growf"] = nc.alloc_sbuf_tensor("growf", [1, 3 * WT], F32)
    st["growbf"] = nc.alloc_sbuf_tensor("growbf", [1, 4 * WT], BF16)
    st["growlo"] = nc.alloc_sbuf_tensor("growlo", [1, WT], F32)
    st["identf"] = nc.alloc_sbuf_tensor("c_identf", [128, 128], F32)
    st["wouth"] = nc.alloc_sbuf_tensor("wouth", [128, 8192], BF16)
    st["psb"] = nc.alloc_psum_tensor("psbf", [128, 1024], BF16)


def build_program():
    nc = bass.Bass("TRN2", target_bir_lowering=False, debug=False)
    ap = {}
    def din(name, shape, dt=F32):
        ap[name] = nc.dram_tensor(name, shape, dt, kind="ExternalInput").ap()
    din("xT", [D, T])
    din("wrT", [D, D]); din("wkT", [D, D]); din("wvT", [D, D]); din("murk", [2, D])
    din("wqkv", [D, 2048]); din("wz", [D, D]); din("wab", [D, 64]); din("wout", [D, D])
    din("dtb", [16]); din("alog", [16]); din("cst", [128, 736])
    x1 = nc.dram_tensor("x1", [D, T], F32, kind="Internal").ap()
    if DBG:
        ap["dbgt"] = nc.dram_tensor("dbgt", [128, 2048 * 4], BF16, kind="ExternalOutput").ap()
        ap["dbgq"] = nc.dram_tensor("dbgq", [128, 16 * T], BF16, kind="ExternalOutput").ap()
        ap["dbgo"] = nc.dram_tensor("dbgo", [128, 16 * 64 * NCP], F32, kind="ExternalOutput").ap()
        ap["dbgn"] = nc.dram_tensor("dbgn", [128, 16 * 64 * NCP], BF16, kind="ExternalOutput").ap()
        ap["dbgs"] = nc.dram_tensor("dbgs", [128, (2 * 192 + 96) * NSL], F32, kind="ExternalOutput").ap()
    x2 = nc.dram_tensor("x2", [D, T], F32, kind="Internal").ap()
    xs2d = nc.dram_tensor("xs2d", [D, T], BF16, kind="Internal").ap()
    out = nc.dram_tensor("out", [D, T], F32, kind="ExternalOutput").ap()
    wpool = nc.alloc_sbuf_tensor("wpool", [128, 25088], BF16)
    po = Pools(nc)

    import os
    ph = os.environ.get("KZ_PH", "123")
    progs = []
    if os.environ.get("KZ_FLAT"):
        po.reset()
        P2 = Prog(nc, "p2")
        st = make_shared(nc, P2)
        extend_shared(nc, st)
        L = build_attn(nc, P2, st, po, x1, x2, xs2d, ap["wqkv"], ap["wz"], ap["wab"],
                       ap["wout"], None, ap["dtb"], ap["alog"], ap["cst"], wpool, "at")
        L["dbg"] = ap if DBG else None
        build_attn_slabs(nc, P2, st, po, L, x1, x2, xs2d, "at")
        P2.finalize()
        P2.replay()
        return nc, {"p2": {e: len(q) for e, q in P2.q.items()}}
    P1 = Prog(nc, "p1")
    st = make_shared(nc, P1)
    extend_shared(nc, st)
    if not os.environ.get("KZ_P1EMPTY"):
        build_ffn(nc, P1, st, po, ap["xT"], x1, ap["wrT"], ap["wkT"], ap["wvT"],
                  ap["murk"], wpool, "f1")
    P1.finalize()
    progs.append(("p1", P1))

    if "2" in ph:
        po.reset()
        P2 = Prog(nc, "p2", entry_waits=progs[-1][1].exit_waits)
        L = build_attn(nc, P2, st, po, x1, x2, xs2d, ap["wqkv"], ap["wz"], ap["wab"],
                       ap["wout"], None, ap["dtb"], ap["alog"], ap["cst"], wpool, "at")
        L["dbg"] = ap if DBG else None
        build_attn_slabs(nc, P2, st, po, L, x1, x2, xs2d, "at")
        P2.finalize()
        progs.append(("p2", P2))

    if "3" in ph:
        po.reset()
        P3 = Prog(nc, "p3", entry_waits=progs[-1][1].exit_waits)
        build_ffn(nc, P3, st, po, x2, out, ap["wrT"], ap["wkT"], ap["wvT"],
                  ap["murk"], wpool, "f3")
        P3.finalize()
        progs.append(("p3", P3))

    for _, pp in progs:
        pp.replay()
    nstats = {t: {e: len(q) for e, q in pp.q.items()} for t, pp in progs}
    return nc, nstats


def host_constants():
    cst = np.zeros((128, 736), np.float32)
    tt_ = np.arange(128)
    same = (tt_[:, None] // 64) == (tt_[None, :] // 64)
    cst[:, 0:128] = np.where(same & (tt_[:, None] > tt_[None, :]), 0.0, -1e30)
    cst[:, 128:256] = np.where(same & (tt_[:, None] <= tt_[None, :]), 0.0, -1e30)
    selA = np.zeros((6, 128), BFD); selA[0:3, :] = 1
    selP = np.zeros((6, 128), BFD); selP[3:6, :] = 1
    cst[0:6, 384:448] = selA.view(np.uint16).astype(np.uint32).view(np.float32) if False else 0
    def pack_bf(dst_rows, col0, arr):
        b = np.ascontiguousarray(arr).view(np.uint16)
        u32 = b[:, 0::2].astype(np.uint32) | (b[:, 1::2].astype(np.uint32) << 16)
        cst.view(np.uint32)[dst_rows, col0:col0 + u32.shape[1]] = u32
    pack_bf(slice(0, 6), 384, selA)
    pack_bf(slice(0, 6), 448, selP)
    selsq = np.zeros((128, 64), BFD)
    for kt in range(4):
        for s_ in range(16):
            kh = kh_of(s_)
            if kh // 2 == kt:
                base = 64 * (kh % 2)
                selsq[base:base + 64, kt * 16 + s_] = 1
    pack_bf(slice(0, 128), 512, selsq)
    pack_bf(slice(0, 128), 544, np.eye(128, dtype=BFD))
    cst[:, 608:736] = np.eye(128, dtype=np.float32)
    return cst


def host_inputs(d):
    perm = SLOT_PERM
    Wq = d["L0_attn_W_qkv"].copy()
    vblk = Wq[:, 1024:].reshape(D, 16, 64)
    Wq[:, 1024:] = vblk[:, perm, :].reshape(D, 1024)
    Wz = d["L0_attn_W_z"].reshape(D, 16, 64)[:, perm, :].reshape(D, D).copy()
    Wab = np.zeros((D, 64), np.float32)
    Wab[:, 0:16] = d["L0_attn_W_a"][:, perm]
    Wab[:, 32:48] = d["L0_attn_W_b"][:, perm]
    Wout = d["L0_attn_W_out"].reshape(16, 64, D)[perm].reshape(D, D).copy()
    convw = d["L0_attn_conv_w"].copy()
    cv = convw[1024:].reshape(16, 64, 4)
    convw[1024:] = cv[perm].reshape(1024, 4)
    dtb = d["L0_attn_dt_bias"][perm].copy()
    alog = d["L0_attn_A_log"][perm].copy()
    cst = host_constants()
    cst[:, 256:320] = np.tile(d["L0_attn_norm_w"], (128, 1))
    cst[:, 320:384] = convw.reshape(16, 128, 4).transpose(1, 0, 2).reshape(128, 64)
    base = {
        "wrT": np.ascontiguousarray(d["L0_ffn_W_r"].T),
        "wkT": np.ascontiguousarray(d["L0_ffn_W_k"].T),
        "wvT": np.ascontiguousarray(d["L0_ffn_W_v"].T),
        "murk": np.ascontiguousarray(np.stack([d["L0_ffn_mu_r"], d["L0_ffn_mu_k"]])),
        "wqkv": np.ascontiguousarray(Wq),
        "wz": np.ascontiguousarray(Wz),
        "wab": Wab,
        "wout": np.ascontiguousarray(Wout),
        "dtb": dtb, "alog": alog, "cst": cst,
    }
    maps = []
    for b in range(8):
        m = dict(base)
        m["xT"] = np.ascontiguousarray(d["L0_attn_hidden_states"][b].T)
        maps.append(m)
    return maps


def kernel(**inputs):
    import os
    from concourse.bass_utils import run_bass_kernel_spmd
    nc, _ = build_program()
    maps = host_inputs(inputs)
    ncores = int(os.environ.get("KZ_CORES", "8"))
    res = run_bass_kernel_spmd(nc, maps[:ncores], core_ids=list(range(ncores)))
    out = np.stack([np.ascontiguousarray(res.results[b]["out"].T)
                    for b in range(ncores)] + [
                    np.zeros((8, 2048, 1024), np.float32)[0]] * (8 - ncores))
    return out
